# version 23 (final)
# baseline (speedup 1.0000x reference)
import numpy as np
import concourse.bass as bass
import concourse.mybir as mybir
from concourse.bass_utils import run_bass_kernel_spmd

F32 = mybir.dt.float32
BF16 = mybir.dt.bfloat16
ALU = mybir.AluOpType
ACT = mybir.ActivationFunctionType
AX = mybir.AxisListType

D = 1024
H = 16
TPC = 2048
NT = TPC // 128
WST = 12
EPS_LN = 1e-5
EPS_ATT = 1e-6
EPS_FFN = 1e-8
SCALE = 0.125


class _Nop:
    def then_inc(self, *a, **k):
        return self


_NOP = _Nop()


class _Ctx:
    def __init__(self):
        self.marks = {}
        self.planning = True
        self.vals = {}

    def reset_vals(self):
        self.vals = {}

    def helpers(self, eng, sems):
        planning = eng is None

        def wait(label):
            if not planning:
                sem, v = self.marks[label]
                eng.wait_ge(sems[sem], v)

        def inc(inst, sem, n, label=None):
            v = self.vals.get(sem, 0) + n
            self.vals[sem] = v
            if label is not None and self.planning:
                assert label not in self.marks, label
                self.marks[label] = (sem, v)
            if not planning:
                inst.then_inc(sems[sem], n)

        def fence(inst, sem):
            v = self.vals.get(sem, 0) + 1
            self.vals[sem] = v
            if not planning:
                inst.then_inc(sems[sem], 1)
                eng.wait_ge(sems[sem], v)

        return wait, inc, fence


def _ap(base_ap, extra_off, dims):
    p = list(base_ap.ap[0])
    return bass.AP(base_ap.tensor, base_ap.offset + extra_off,
                   [p] + [list(d) for d in dims])


def build_nc():
    nc = bass.Bass("TRN2", target_bir_lowering=False, debug=False)

    x_h = nc.dram_tensor("x", [TPC, D], F32, kind="ExternalInput").ap()
    wq_h = nc.dram_tensor("wqkv", [3 * D, D], F32, kind="ExternalInput").ap()
    wf_h = nc.dram_tensor("wffn", [D, D], F32, kind="ExternalInput").ap()
    y_h = nc.dram_tensor("y", [TPC, D], F32, kind="ExternalOutput").ap()
    cc_in = nc.dram_tensor("cc_in", [128, 1040], BF16, kind="Internal")
    cc_out = nc.dram_tensor("cc_out", [128, 1040], BF16, kind="Internal")

    guards = []

    def alloc(name, shape, dtype, side=None, td=False):
        g = nc.sbuf_tensor(name, shape, dtype, side=side,
                           is_transpose_dest=td)
        t = g.__enter__()
        guards.append(g)
        return t

    wqT = alloc("wqT", [128, 8192], BF16, "left", td=True)
    q_fm = alloc("q_fm", [128, H * 1024], BF16, "left", td=True)
    arena = alloc("arena", [128, 4096], BF16, "left")
    x_in = alloc("x_in", [128, 3072], F32, "left")
    x1T = alloc("x1T", [128, 3072], BF16, "left", td=True)
    scr1 = alloc("scr1", [128, 2048], F32, "left")
    scr2 = alloc("scr2", [128, 2048], F32, "left")
    stats = alloc("stats", [128, 16], F32, "left")
    stats2 = alloc("stats2", [128, 128], F32, "left")
    wcol = alloc("wcol", [128, 16], F32, "left")
    colsB = alloc("colsB", [128, 48], F32, "left")
    ones_sq = alloc("ones_sq", [128, 128], F32, "left")
    kv_sb = alloc("kv_sb", [128, 1040], F32, "left")
    kv_bd = alloc("kv_bd", [128, 1040], BF16, "left")
    aqbuf = alloc("aqbuf", [128, 2048], BF16, "left")

    gA2 = []

    def allocA2(name, shape, dtype, td=False):
        g = nc.sbuf_tensor(name, shape, dtype, side="left",
                           is_transpose_dest=td)
        t = g.__enter__()
        gA2.append(g)
        return t

    qc = allocA2("qc", [128, 4096], BF16)
    sq_q = allocA2("sq_q", [128, 2048], BF16)
    sq_k = allocA2("sq_k", [128, 2048], BF16)
    qn = allocA2("qn", [128, 2048], BF16)
    kn = allocA2("kn", [128, 2048], BF16)
    v_ext = allocA2("v_ext", [128, 4160], BF16)
    x1 = allocA2("x1", [128, 2048], BF16)
    WfA = allocA2("WfA", [128, 2048], F32)

    gA1 = nc.sbuf_tensor("W_T", [128, 24576], BF16, side="right",
                         is_transpose_dest=True)
    W_T = gA1.__enter__()

    gA1.__exit__(None, None, None)
    Wf2 = alloc("Wf2", [128, 8192], F32, "right")
    attn_s = alloc("attn_s", [128, 2048], F32, "right")
    y_out = alloc("y_out", [128, 2048], F32, "right")

    for g in reversed(gA2):
        g.__exit__(None, None, None)
    x_mid = alloc("x_mid", [128, 7168], F32, "left")

    bpn = arena

    try:
        P_g = nc.psum_tensor("P", [128, 4096], F32)
        P = P_g.__enter__()
        guards.append(P_g)
        Pap = P[:]

        sem_names = ["SX", "SG", "PE", "DV", "AC", "GP", "CC", "ST"]
        sem_gs = {n: nc.semaphore(n) for n in sem_names}
        sems = {n: g.__enter__() for n, g in sem_gs.items()}
        guards.extend(sem_gs.values())
        blk_g = nc.Block()
        blk = blk_g.__enter__()
        guards.append(blk_g)

        KVBASE = [2560 + 512 * (g // 3) + 160 * (g % 3) for g in range(8)]


        def prog_sync(eng, C):
            wait, inc, fence = C.helpers(eng, sems)
            live = eng is not None

            def DMA(label, out, in_, sem="SX", **kw):
                i = eng.dma_start(out=out, in_=in_, **kw) if live else _NOP
                inc(i, sem, 16, label)

            wait(("vinit",))

            def do_xld(n, labels=("xld",)):
                if n >= 3:
                    wait(("ln1", n - 3))
                DMA((labels[0], n),
                    x_in[:, 1024 * (n % 3):1024 * (n % 3) + 1024],
                    x_h[128 * n:128 * (n + 1), :])

            def do_x1T(t):
                wait(("ln1", t))
                if t >= 6:
                    wait(("vmm", (t - 6, 1)))
                x1_3 = x1[:, 1024 * (t % 2):1024 * (t % 2) + 1024] \
                    .rearrange("p (k f) -> p k f", k=8)
                x1T_3 = x1T[:, 1024 * (t % 3):1024 * (t % 3) + 1024] \
                    .rearrange("p (k f) -> p k f", k=8)
                DMA(("x1T", t), x1T_3, x1_3, transpose=True)

            def do_wT(s):
                wait(("wload", s))
                for oj in range(2):
                    a_in = arena[:, 2048 * (s % 2) + 1024 * oj:
                                 2048 * (s % 2) + 1024 * (oj + 1)] \
                        .rearrange("p (cc c) -> p cc c", cc=8)
                    o_out = _ap(W_T[:], 256 * s + 128 * oj,
                                [[3072, 8], [1, 128]]) if live else None
                    i = eng.dma_start(out=o_out, in_=a_in,
                                      transpose=True) if live else _NOP
                    inc(i, "SX", 16, ("wT", s) if oj == 1 else None)

            do_xld(0)
            do_xld(1)
            do_xld(2)
            do_wT(0)
            do_wT(1)
            do_xld(3)
            do_x1T(0)
            do_wT(2)
            do_wT(3)
            do_xld(4)
            do_x1T(1)
            do_wT(4)
            do_wT(5)
            do_xld(5)
            do_x1T(2)
            for s in range(6, WST):
                do_wT(s)

            for n in range(6, NT + 8):
                if n < NT:
                    do_xld(n)
                t = n - 3
                if 0 <= t < NT:
                    do_x1T(t)
                t = n - 8
                if 0 <= t < NT:
                    wait(("qn", t))
                    qn_3 = qn[:, 1024 * (t % 2):1024 * (t % 2) + 1024] \
                        .rearrange("p (k f) -> p k f", k=8)
                    qfm_3 = _ap(q_fm[:], 128 * t,
                                [[2048, 8], [1, 128]]) if live else None
                    i2 = eng.dma_start(out=qfm_3, in_=qn_3, transpose=True) \
                        if live else _NOP
                    inc(i2, "SX", 16, ("qT", t))

            for n in range(3):
                wait(("ln1", [15, 13, 14][n]))
                DMA(("xldB", n),
                    x_in[:, 1024 * (n % 3):1024 * (n % 3) + 1024],
                    x_h[128 * n:128 * (n + 1), :])

            for h2 in range(2):
                wait(("wq", 4 * h2 + 3))
                for oc in range(4):
                    a_in = arena[:, 1024 * oc:1024 * (oc + 1)].rearrange(
                        "p (ii c) -> p ii c", ii=8)
                    o_out = _ap(wqT[:], 512 * h2 + 128 * oc,
                                [[1024, 8], [1, 128]]) if live else None
                    i = eng.dma_start(out=o_out, in_=a_in, transpose=True) \
                        if live else _NOP
                    inc(i, "SX", 16, ("wqT", h2) if oc == 3 else None)

            for n in range(3, NT + 8):
                if n < NT:
                    wait(("xmid", n - 3))
                    DMA(("xldB", n),
                        x_in[:, 1024 * (n % 3):1024 * (n % 3) + 1024],
                        x_h[128 * n:128 * (n + 1), :])
                t = n - 5
                if 0 <= t < NT:
                    wait(("aq", t))
                    if t >= 3:
                        wait(("ffnmm", t - 3))
                    else:
                        wait(("vmm", ([15, 13, 14][t], 1)))
                    aq_3 = aqbuf[:, 1024 * (t % 2):1024 * (t % 2) + 1024] \
                        .rearrange("p (k f) -> p k f", k=8)
                    aqT_3 = x1T[:, 1024 * (t % 3):1024 * (t % 3) + 1024] \
                        .rearrange("p (k f) -> p k f", k=8)
                    DMA(("aqT", t), aqT_3, aq_3, transpose=True)
                t = n - 8
                if 0 <= t < NT:
                    wait(("yout", t))
                    DMA(("yst", t), y_h[128 * t:128 * (t + 1), :],
                        y_out[:, 1024 * (t % 2):1024 * (t % 2) + 1024],
                        sem="ST")
            if live:
                eng.wait_ge(sems["ST"], 16 * NT)

        def prog_gpsimd(eng, C):
            wait, inc, fence = C.helpers(eng, sems)
            live = eng is not None

            for s in range(WST):
                if s >= 2:
                    wait(("wT", s - 2))
                src = wq_h[256 * s:256 * (s + 1), :].rearrange(
                    "(oj p) c -> p oj c", p=128)
                dst = arena[:, 2048 * (s % 2):2048 * (s % 2 + 1)].rearrange(
                    "p (oj c) -> p oj c", oj=2)
                i = eng.dma_start(out=dst, in_=src) if live else _NOP
                inc(i, "SG", 16, ("wload", s))

            for j in range(8):
                if j >= 2:
                    wait(("wsum", j - 2))
                i = eng.dma_start(out=WfA[:, 1024 * (j % 2):
                                          1024 * (j % 2) + 1024],
                                  in_=wf_h[128 * j:128 * (j + 1), :]) \
                    if live else _NOP
                inc(i, "SG", 16, ("wfld", j))

            wait(("qkmm", NT - 1))
            wait(("vmm", (NT - 1, 0)))
            wait(("vmm", (NT - 1, 1)))
            for j in range(8):
                i = eng.dma_start(out=Wf2[:, 1024 * j:1024 * (j + 1)],
                                  in_=wf_h[128 * j:128 * (j + 1), :]) \
                    if live else _NOP
                inc(i, "SG", 16, ("wfld2", j))

            wait(("qT", NT - 1))
            wait(("kvacc", NT - 1))
            i = eng.dma_start(out=cc_in.ap(), in_=kv_sb[:]) if live else _NOP
            inc(i, "SG", 16, ("ccin",))
            wait(("ccin",))
            if live:
                cc = eng.collective_compute(
                    "AllReduce", ALU.add,
                    replica_groups=[[0, 1], [2, 3], [4, 5], [6, 7]],
                    ins=[cc_in.ap().opt()],
                    outs=[cc_out.ap().opt()],
                )
                cc.then_inc(sems["CC"], 1)
                eng.wait_ge(sems["CC"], 1)
            i = eng.dma_start(out=kv_sb[:], in_=cc_out.ap()) if live else _NOP
            inc(i, "SG", 16, ("kvtot",))

            for n in range(1, NT + 5):
                t = n - 1
                if 0 <= t < NT:
                    wait(("attn_sc", t))
                    wait(("xldB", t))
                    if t >= 7:
                        wait(("yout", t - 7))
                    i = eng.tensor_add(x_mid[:, 1024 * (t % 7):
                                             1024 * (t % 7) + 1024],
                                       attn_s[:, 1024 * (t % 2):
                                              1024 * (t % 2) + 1024],
                                       x_in[:, 1024 * (t % 3):
                                            1024 * (t % 3) + 1024]) \
                        if live else _NOP
                    inc(i, "GP", 1, ("xmid", t))


        def prog_tensor(eng, C):
            wait, inc, fence = C.helpers(eng, sems)
            live = eng is not None

            def MM(out, lhsT, rhs, start, stop):
                return eng.matmul(out, lhsT, rhs, start=start, stop=stop) \
                    if live else _NOP

            for n in range(NT + 8):
                t = n - 5
                if 0 <= t < NT:
                    wait(("vextA", t))
                    wait(("x1T", t))
                    wait(("wT", 11))
                    last = _NOP
                    for ci in range(8):
                        last = MM(Pap[:, 2048:2560],
                                  x1T[:, 1024 * (t % 3) + 128 * ci:]
                                  [:, 0:128],
                                  W_T[:, 3072 * ci + 2560:][:, 0:512],
                                  ci == 0, ci == 7)
                    inc(last, "PE", 1, ("vmm", (t, 1)))
                t = n - 8
                if 0 <= t < NT:
                    wait(("kn", t))
                    wait(("vextB", t))
                    if t >= 1:
                        wait(("kvacc", t - 1))
                    last = _NOP
                    for g in range(8):
                        last = MM(Pap[:, KVBASE[g]:KVBASE[g] + 129],
                                  kn[:, 1024 * (t % 2) + 128 * g:][:, 0:128],
                                  v_ext[:, 1040 * (t % 4) + 130 * g:]
                                  [:, 0:129],
                                  True, True)
                    inc(last, "PE", 1, ("kvmm", t))
                t = n - 4
                if 0 <= t < NT:
                    wait(("x1T", t))
                    wait(("wT", 9))
                    if t >= 1:
                        wait(("vextB", t - 1))
                    last = _NOP
                    for ci in range(8):
                        last = MM(Pap[:, 2048:2560],
                                  x1T[:, 1024 * (t % 3) + 128 * ci:]
                                  [:, 0:128],
                                  W_T[:, 3072 * ci + 2048:][:, 0:512],
                                  ci == 0, ci == 7)
                    inc(last, "PE", 1, ("vmm", (t, 0)))
                    if t >= 1:
                        wait(("evq", t - 1))
                        wait(("evk", t - 1))
                    last = _NOP
                    for jj in range(4):
                        wait(("wT", 2 * jj + 1))
                        for ci in range(8):
                            last = MM(Pap[:, 512 * jj:512 * (jj + 1)],
                                      x1T[:, 1024 * (t % 3) + 128 * ci:]
                                      [:, 0:128],
                                      W_T[:, 3072 * ci + 512 * jj:]
                                      [:, 0:512],
                                      ci == 0, ci == 7)
                    inc(last, "PE", 1, ("qkmm", t))

            wait(("wsums",))
            wait(("kvacc", NT - 1))
            m = MM(Pap[:, 4000:4001], ones_sq[:], wcol[:, 0:1], True, True)
            inc(m, "PE", 1, ("wsmm",))

            for n in range(NT + 6):
                t = n - 6
                if 0 <= t < NT:
                    wait(("aqT", t))
                    if t == 0:
                        wait(("wqT", 0))
                        wait(("wqT", 1))
                    if t >= 2:
                        wait(("yout", t - 2))
                    last = _NOP
                    for jj in range(2):
                        for ci in range(8):
                            last = MM(Pap[:, 2048 + 1024 * (t % 2) +
                                          512 * jj:][:, 0:512],
                                      x1T[:, 1024 * (t % 3) + 128 * ci:]
                                      [:, 0:128],
                                      wqT[:, 1024 * ci + 512 * jj:]
                                      [:, 0:512],
                                      ci == 0, ci == 7)
                    inc(last, "PE", 1, ("ffnmm", t))
                t = n
                if t < NT:
                    if t == 0:
                        wait(("kvbd",))
                    wait(("qT", t))
                    if t >= 1:
                        wait(("attn_sc", t - 1))
                    last = _NOP
                    for g in range(8):
                        last = MM(Pap[:, 256 * g:256 * g + 130],
                                  q_fm[:, 2048 * g + 128 * t:][:, 0:128],
                                  kv_bd[:, 130 * g:130 * (g + 1)],
                                  True, True)
                    inc(last, "PE", 1, ("outmm", t))

        def prog_scalar(eng, C):
            wait, inc, fence = C.helpers(eng, sems)
            live = eng is not None

            def A(out, in_, func, **kw):
                return eng.activation(out, in_, func, **kw) if live else _NOP

            for n in range(NT + 7):
                t = n - 5
                if 0 <= t < NT:
                    wait(("qkmm", t))
                    if t >= 2:
                        wait(("kln", t - 2))
                    a = A(qc[:, 2048 * (t % 2) + 1024:2048 * (t % 2) + 2048],
                          Pap[:, 1024:2048], ACT.Copy)
                    inc(a, "AC", 1, ("evk", t))
                t = n - 2
                if 0 <= t < NT:
                    wait(("var", t))
                    A(stats[:, 8 * (t % 2) + 6:8 * (t % 2) + 7],
                      stats[:, 8 * (t % 2) + 5:8 * (t % 2) + 6],
                      ACT.Ln, bias=wcol[:, 13:14], scale=wcol[:, 1:2])
                t = n - 7
                if 0 <= t < NT:
                    wait(("qln", t))
                    A(scr1[:, 1024 * (t % 2):1024 * (t % 2) + 1024],
                      scr1[:, 1024 * (t % 2):1024 * (t % 2) + 1024], ACT.Exp)
                    wait(("kln", t))
                    A(scr2[:, 1024 * (t % 2):1024 * (t % 2) + 1024],
                      scr2[:, 1024 * (t % 2):1024 * (t % 2) + 1024], ACT.Exp)
                t = n - 2
                if 0 <= t < NT:
                    a = A(stats[:, 8 * (t % 2) + 7:8 * (t % 2) + 8],
                          stats[:, 8 * (t % 2) + 6:8 * (t % 2) + 7],
                          ACT.Exp, scale=wcol[:, 15:16])
                    inc(a, "AC", 1, ("inv1", t))
                t = n - 1
                if 0 <= t < NT:
                    wait(("xld", t))
                    if t >= 2:
                        wait(("var", t - 2))
                    a = A(aqbuf[:, 0:1024],
                          x_in[:, 1024 * (t % 3):1024 * (t % 3) + 1024],
                          ACT.Square,
                          accum_out=stats[:, 8 * (t % 2) + 1:
                                          8 * (t % 2) + 2])
                    inc(a, "AC", 1, ("s2", t))
                t = n - 6
                if 0 <= t < NT:
                    wait(("ssk", t))
                    if t >= 2:
                        wait(("kln", t - 2))
                    s2B = 64 * (t % 2)
                    A(stats2[:, s2B + 16:s2B + 32], stats2[:, s2B:s2B + 16],
                      ACT.Ln)
                    A(stats2[:, s2B + 48:s2B + 64],
                      stats2[:, s2B + 32:s2B + 48], ACT.Ln)
                t = n - 7
                if 0 <= t < NT:
                    if t >= 2:
                        wait(("qT", t - 2))
                    a = A(qn[:, 1024 * (t % 2):1024 * (t % 2) + 1024],
                          scr1[:, 1024 * (t % 2):1024 * (t % 2) + 1024],
                          ACT.Ln, bias=wcol[:, 14:15])
                    inc(a, "AC", 1, ("qn", t))
                    if t >= 2:
                        wait(("kvmm", t - 2))
                    a = A(kn[:, 1024 * (t % 2):1024 * (t % 2) + 1024],
                          scr2[:, 1024 * (t % 2):1024 * (t % 2) + 1024],
                          ACT.Ln, bias=wcol[:, 14:15])
                    inc(a, "AC", 1, ("kn", t))
                t = n - 6
                if 0 <= t < NT:
                    s2B = 64 * (t % 2)
                    A(stats2[:, s2B + 16:s2B + 32],
                      stats2[:, s2B + 16:s2B + 32], ACT.Exp,
                      scale=wcol[:, 15:16])
                    a = A(stats2[:, s2B + 48:s2B + 64],
                          stats2[:, s2B + 48:s2B + 64], ACT.Exp,
                          scale=wcol[:, 15:16])
                    inc(a, "AC", 1, ("invqk", t))
                t = n - 5
                if 0 <= t < NT:
                    wait(("evq", t))
                    if t >= 2:
                        wait(("ssk", t - 2))
                    a = A(sq_q[:, 1024 * (t % 2):1024 * (t % 2) + 1024],
                          qc[:, 2048 * (t % 2):2048 * (t % 2) + 1024],
                          ACT.Square)
                    inc(a, "AC", 1, ("sqq", t))
                    a = A(sq_k[:, 1024 * (t % 2):1024 * (t % 2) + 1024],
                          qc[:, 2048 * (t % 2) + 1024:2048 * (t % 2) + 2048],
                          ACT.Square)
                    inc(a, "AC", 1, ("sqk", t))

            for n in range(NT + 4):
                t = n - 3
                if 0 <= t < NT:
                    wait(("varB", t))
                    A(colsB[:, 30 + t % 2:31 + t % 2],
                      colsB[:, 28 + t % 2:29 + t % 2],
                      ACT.Ln, bias=wcol[:, 13:14], scale=wcol[:, 1:2])
                t = n - 2
                if 0 <= t < NT:
                    wait(("xmid", t))
                    if t >= 1:
                        wait(("varB", t - 1))
                    A(attn_s[:, 1024 * (t % 2):1024 * (t % 2) + 1024],
                      x_mid[:, 1024 * (t % 7):1024 * (t % 7) + 1024],
                      ACT.Copy, accum_out=colsB[:, 0:1])
                    a = A(attn_s[:, 1024 * (t % 2):1024 * (t % 2) + 1024],
                          x_mid[:, 1024 * (t % 7):1024 * (t % 7) + 1024],
                          ACT.Square, accum_out=colsB[:, 1:2])
                    inc(a, "AC", 1, ("s2B", t))
                t = n - 3
                if 0 <= t < NT:
                    fence(_NOP if not live else
                          eng.activation(colsB[:, 38:39], wcol[:, 14:15],
                                         ACT.Copy), "AC")
                    a = A(colsB[:, 6 + t % 2:7 + t % 2],
                          colsB[:, 30 + t % 2:31 + t % 2],
                          ACT.Exp, scale=wcol[:, 15:16])
                    inc(a, "AC", 1, ("inv2", t))
                    if t >= 2:
                        wait(("z", t - 2))
                    a = A(scr1[:, 1024 * (t % 2):1024 * (t % 2) + 1024],
                          x_mid[:, 1024 * (t % 7):1024 * (t % 7) + 1024],
                          ACT.Identity, bias=colsB[:, 4 + t % 2:5 + t % 2])
                    fence(a, "AC")
                    a = A(scr1[:, 1024 * (t % 2):1024 * (t % 2) + 1024],
                          scr1[:, 1024 * (t % 2):1024 * (t % 2) + 1024],
                          ACT.Copy, scale=colsB[:, 6 + t % 2:7 + t % 2])
                    inc(a, "AC", 1, ("x2", t))

        def prog_vector(eng, C):
            wait, inc, fence = C.helpers(eng, sems)
            live = eng is not None

            def O(fn, *a, **kw):
                return fn(*a, **kw) if live else _NOP

            v = eng
            if live:
                v.memset(wcol[:, 13:14], EPS_LN)
                v.memset(wcol[:, 1:2], -1.0)
                v.memset(wcol[:, 14:15], 1.0)
                v.memset(wcol[:, 15:16], -0.5)
                v.memset(ones_sq[:], 1.0)
                v.memset(kv_sb[:], 0.0)
                iz = v.memset(kv_bd[:], 0.0)
                fence(iz, "DV")
                ve3 = v_ext[:].rearrange("p (s g c) -> p s g c", s=4, g=8)
                i1 = v.memset(ve3[:, :, :, 128:129], 1.0)
            else:
                fence(_NOP, "DV")
                i1 = _NOP
            inc(i1, "DV", 1, ("vinit",))

            for n in range(NT + 10):
                t = n - 9
                if 0 <= t < NT:
                    wait(("kvmm", t))
                    if live:
                        for (poff, koff, ng) in ((2560, 0, 3), (3072, 390, 3),
                                                 (3584, 780, 2)):
                            pin = _ap(Pap, poff, [[160, ng], [1, 129]])
                            acc = kv_sb[:, koff:koff + 130 * ng].rearrange(
                                "p (g c) -> p g c", g=ng)[:, :, 0:129]
                            i1 = v.scalar_tensor_tensor(acc, pin, 1.0, acc,
                                                        ALU.mult, ALU.add)
                    else:
                        i1 = _NOP
                    inc(i1, "DV", 1, ("kvacc", t))
                t = n - 5
                if 0 <= t < NT:
                    wait(("qkmm", t))
                    if t >= 2:
                        wait(("sqk", t - 2))
                    i1 = O(v.tensor_copy if live else None,
                           qc[:, 2048 * (t % 2):2048 * (t % 2) + 1024],
                           Pap[:, 0:1024]) if live else _NOP
                    inc(i1, "DV", 1, ("evq", t))
                    wait(("vmm", (t, 1)))
                    if live:
                        src = Pap[:, 2048:2560].rearrange(
                            "p (g d) -> p g d", g=4)
                        dst = _ap(v_ext[:], 1040 * (t % 4) + 520,
                                  [[130, 4], [1, 128]])
                        i1 = v.tensor_copy(dst, src)
                    else:
                        i1 = _NOP
                    inc(i1, "DV", 1, ("vextB", t))
                t = n - 6
                if 0 <= t < NT:
                    s2B = 64 * (t % 2)
                    wait(("sqq", t))
                    O(v.reduce_sum if live else None,
                      stats2[:, s2B:s2B + 16],
                      sq_q[:, 1024 * (t % 2):1024 * (t % 2) + 1024]
                      .rearrange("p (h d) -> p h d", h=16), axis=AX.X)
                    wait(("sqk", t))
                    i1 = O(v.reduce_sum if live else None,
                           stats2[:, s2B + 32:s2B + 48],
                           sq_k[:, 1024 * (t % 2):1024 * (t % 2) + 1024]
                           .rearrange("p (h d) -> p h d", h=16), axis=AX.X) \
                        if live else _NOP
                    inc(i1, "DV", 1, ("ssk", t))
                t = n - 1
                if 0 <= t < NT:
                    wait(("xld", t))
                    i1 = O(v.reduce_sum if live else None,
                           stats[:, 8 * (t % 2):8 * (t % 2) + 1],
                           x_in[:, 1024 * (t % 3):1024 * (t % 3) + 1024],
                           axis=AX.X) if live else _NOP
                    fence(i1, "DV")
                j = n - 6
                if 0 <= j < 8:
                    wait(("wfld", j))
                    i1 = O(v.reduce_sum if live else None,
                           wcol[:, 2 + j:3 + j],
                           WfA[:, 1024 * (j % 2):1024 * (j % 2) + 1024],
                           axis=AX.X, apply_absolute_value=True) \
                        if live else _NOP
                    inc(i1, "DV", 1, ("wsum", j))
                t = n - 1
                if 0 <= t < NT:
                    wait(("s2", t))
                    if live:
                        v.tensor_scalar(stats[:, 8 * (t % 2) + 3:
                                              8 * (t % 2) + 4],
                                        stats[:, 8 * (t % 2) + 1:
                                              8 * (t % 2) + 2],
                                        1.0 / D, None, ALU.mult)
                        v.tensor_scalar(stats[:, 8 * (t % 2) + 2:
                                              8 * (t % 2) + 3],
                                        stats[:, 8 * (t % 2):
                                              8 * (t % 2) + 1],
                                        1.0 / D, None, ALU.mult)
                t = n - 6
                if 0 <= t < NT:
                    wait(("invqk", t))
                    s2B = 64 * (t % 2)
                    i1 = O(v.tensor_tensor if live else None,
                           scr1[:, 1024 * (t % 2):1024 * (t % 2) + 1024]
                           .rearrange("p (h d) -> p h d", h=16),
                           qc[:, 2048 * (t % 2):2048 * (t % 2) + 1024]
                           .rearrange("p (h d) -> p h d", h=16),
                           stats2[:, s2B + 16:s2B + 32].unsqueeze(2)
                           .broadcast_to([128, 16, 64]),
                           ALU.mult) if live else _NOP
                    inc(i1, "DV", 1, ("qln", t))
                    i1 = O(v.tensor_tensor if live else None,
                           scr2[:, 1024 * (t % 2):1024 * (t % 2) + 1024]
                           .rearrange("p (h d) -> p h d", h=16),
                           qc[:, 2048 * (t % 2) + 1024:
                              2048 * (t % 2) + 2048]
                           .rearrange("p (h d) -> p h d", h=16),
                           stats2[:, s2B + 48:s2B + 64].unsqueeze(2)
                           .broadcast_to([128, 16, 64]),
                           ALU.mult) if live else _NOP
                    inc(i1, "DV", 1, ("kln", t))
                t = n - 2
                if 0 <= t < NT:
                    wait(("inv1", t))
                    if t >= 2:
                        wait(("x1T", t - 2))
                    i1 = O(v.tensor_scalar if live else None,
                           x1[:, 1024 * (t % 2):1024 * (t % 2) + 1024],
                           x_in[:, 1024 * (t % 3):1024 * (t % 3) + 1024],
                           stats[:, 8 * (t % 2) + 2:8 * (t % 2) + 3],
                           stats[:, 8 * (t % 2) + 7:8 * (t % 2) + 8],
                           ALU.subtract, ALU.mult) if live else _NOP
                    inc(i1, "DV", 1, ("ln1", t))
                t = n - 1
                if 0 <= t < NT:
                    fence(O(v.engine_nop if live else None)
                          if live else _NOP, "DV")
                    i1 = O(v.scalar_tensor_tensor if live else None,
                           stats[:, 8 * (t % 2) + 5:8 * (t % 2) + 6],
                           stats[:, 8 * (t % 2) + 2:8 * (t % 2) + 3],
                           stats[:, 8 * (t % 2) + 2:8 * (t % 2) + 3],
                           stats[:, 8 * (t % 2) + 3:8 * (t % 2) + 4],
                           ALU.mult, ALU.subtract) if live else _NOP
                    inc(i1, "DV", 1, ("var", t))
                t = n - 4
                if 0 <= t < NT:
                    wait(("vmm", (t, 0)))
                    if t >= 4:
                        wait(("kvmm", t - 4))
                    if live:
                        src = Pap[:, 2048:2560].rearrange(
                            "p (g d) -> p g d", g=4)
                        dst = _ap(v_ext[:], 1040 * (t % 4),
                                  [[130, 4], [1, 128]])
                        i1 = v.tensor_copy(dst, src)
                    else:
                        i1 = _NOP
                    inc(i1, "DV", 1, ("vextA", t))

            i1 = O(v.reduce_sum if live else None, wcol[:, 0:1],
                   wcol[:, 2:10], axis=AX.X) if live else _NOP
            inc(i1, "DV", 1, ("wsums",))
            wait(("wsmm",))
            if live:
                iw = v.tensor_scalar(wcol[:, 10:11], Pap[:, 4000:4001],
                                     1.0 / (1024.0 * 1024.0), EPS_FFN,
                                     ALU.mult, ALU.add)
                fence(iw, "DV")
                v.tensor_scalar(wcol[:, 11:12], wcol[:, 10:11], 0.5, None,
                                ALU.mult)
                iw = v.tensor_scalar(wcol[:, 12:13], wcol[:, 10:11], -0.5,
                                     None, ALU.mult)
                fence(iw, "DV")
            else:
                fence(_NOP, "DV")
                fence(_NOP, "DV")
            for j in range(4):
                wait(("wfld2", j))
                if live:
                    v.tensor_scalar(scr1[:, 0:1024],
                                    Wf2[:, 1024 * j:1024 * (j + 1)],
                                    wcol[:, 11:12], 0.0,
                                    ALU.subtract, ALU.is_gt)
                    v.tensor_scalar(scr2[:, 0:1024],
                                    Wf2[:, 1024 * j:1024 * (j + 1)],
                                    wcol[:, 12:13], 0.0,
                                    ALU.subtract, ALU.is_lt)
                    i1 = v.tensor_sub(
                        arena[:, 1024 * j:1024 * (j + 1)],
                        scr1[:, 0:1024], scr2[:, 0:1024])
                else:
                    i1 = _NOP
                inc(i1, "DV", 1, ("wq", j))
            wait(("kvtot",))
            if live:
                bd = kv_bd[:].rearrange("p (g c) -> p g c", g=8)
                sbk = kv_sb[:].rearrange("p (g c) -> p g c", g=8)
                v.tensor_scalar(bd[0:64, :, 0:64], sbk[0:64, :, 0:64],
                                SCALE, None, ALU.mult)
                v.tensor_scalar(bd[64:128, :, 64:128],
                                sbk[64:128, :, 64:128], SCALE, None,
                                ALU.mult)
                v.tensor_scalar(bd[0:64, :, 128:129], sbk[0:64, :, 128:129],
                                EPS_ATT, None, ALU.add)
                i1 = v.tensor_scalar(bd[64:128, :, 129:130],
                                     sbk[64:128, :, 128:129],
                                     EPS_ATT, None, ALU.add)
            else:
                i1 = _NOP
            inc(i1, "DV", 1, ("kvbd",))
            for j in range(4, 8):
                wait(("wfld2", j))
                if j >= 4:
                    wait(("wqT", 0))
                if live:
                    v.tensor_scalar(scr1[:, 0:1024],
                                    Wf2[:, 1024 * j:1024 * (j + 1)],
                                    wcol[:, 11:12], 0.0,
                                    ALU.subtract, ALU.is_gt)
                    v.tensor_scalar(scr2[:, 0:1024],
                                    Wf2[:, 1024 * j:1024 * (j + 1)],
                                    wcol[:, 12:13], 0.0,
                                    ALU.subtract, ALU.is_lt)
                    i1 = v.tensor_sub(
                        arena[:, 1024 * (j % 4):1024 * (j % 4 + 1)],
                        scr1[:, 0:1024], scr2[:, 0:1024])
                else:
                    i1 = _NOP
                inc(i1, "DV", 1, ("wq", j))

            for n in range(NT + 8):
                t = n - 1
                if 0 <= t < NT:
                    wait(("outmm", t))
                    if live:
                        ir = v.reciprocal(colsB[:, 8:24],
                                          _ap(Pap, 128, [[256, 8], [1, 2]]))
                        fence(ir, "DV")
                    else:
                        fence(_NOP, "DV")
                t = n - 1
                if 0 <= t < NT:
                    if t >= 2:
                        wait(("s2B", t - 2))
                    i1 = O(v.tensor_tensor if live else None,
                           attn_s[:, 1024 * (t % 2):1024 * (t % 2) + 1024]
                           .rearrange("p (g w d) -> p g w d", g=8, w=2),
                           _ap(Pap, 0, [[256, 8], [64, 2], [1, 64]]),
                           colsB[:, 8:24].rearrange("p (g w) -> p g w", g=8)
                           .unsqueeze(3).broadcast_to([128, 8, 2, 64]),
                           ALU.mult) if live else _NOP
                    inc(i1, "DV", 1, ("attn_sc", t))
                t = n - 7
                if 0 <= t < NT:
                    wait(("ffnmm", t))
                    if t >= 2:
                        wait(("yst", t - 2))
                    i1 = O(v.scalar_tensor_tensor if live else None,
                           y_out[:, 1024 * (t % 2):1024 * (t % 2) + 1024],
                           Pap[:, 2048 + 1024 * (t % 2):
                               2048 + 1024 * (t % 2) + 1024],
                           colsB[:, 24 + t % 4:25 + t % 4],
                           x_mid[:, 1024 * (t % 7):1024 * (t % 7) + 1024],
                           ALU.mult, ALU.add) if live else _NOP
                    inc(i1, "DV", 1, ("yout", t))
                t = n - 4
                if 0 <= t < NT:
                    O(v.tensor_scalar if live else None,
                      colsB[:, 32 + t % 2:33 + t % 2],
                      colsB[:, 2 + t % 2:3 + t % 2],
                      EPS_FFN, None, ALU.add)
                t = n - 2
                if 0 <= t < NT:
                    wait(("s2B", t))
                    O(v.tensor_scalar if live else None,
                      colsB[:, 4 + t % 2:5 + t % 2], colsB[:, 0:1],
                      -1.0 / D, None, ALU.mult)
                    i1 = O(v.tensor_scalar if live else None,
                           colsB[:, 36:37], colsB[:, 1:2],
                           1.0 / D, None, ALU.mult) if live else _NOP
                    fence(i1, "DV")
                t = n - 4
                if 0 <= t < NT:
                    if not 0 <= n - 2 < NT:
                        fence(O(v.engine_nop if live else None)
                              if live else _NOP, "DV")
                    O(v.reciprocal if live else None,
                      colsB[:, 34 + t % 2:35 + t % 2],
                      colsB[:, 32 + t % 2:33 + t % 2])
                    i1 = O(v.tensor_scalar if live else None,
                           colsB[:, 24 + t % 4:25 + t % 4],
                           colsB[:, 32 + t % 2:33 + t % 2], wcol[:, 10:11],
                           1.0 / 127.0, ALU.mult, ALU.mult) if live else _NOP
                    fence(i1, "DV")
                t = n - 2
                if 0 <= t < NT:
                    i1 = O(v.scalar_tensor_tensor if live else None,
                           colsB[:, 28 + t % 2:29 + t % 2],
                           colsB[:, 4 + t % 2:5 + t % 2],
                           colsB[:, 4 + t % 2:5 + t % 2],
                           colsB[:, 36:37],
                           ALU.mult, ALU.subtract) if live else _NOP
                    inc(i1, "DV", 1, ("varB", t))
                t = n - 4
                if 0 <= t < NT:
                    if not 0 <= n - 2 < NT:
                        fence(O(v.engine_nop if live else None)
                              if live else _NOP, "DV")
                    i1 = O(v.tensor_scalar if live else None,
                           scr2[:, 0:1024],
                           scr1[:, 1024 * (t % 2):1024 * (t % 2) + 1024],
                           colsB[:, 34 + t % 2:35 + t % 2],
                           127.0, ALU.mult, ALU.mult) if live else _NOP
                    fence(i1, "DV")
                    inc(_NOP if not live else v.engine_nop(), "DV", 1,
                        ("z", t))
                    if t == 0:
                        wait(("wqT", 1))
                    if live:
                        v.tensor_scalar(bpn[:, 0:1024], scr2[:, 0:1024],
                                        0.0, 128.0, ALU.max, ALU.add)
                        v.tensor_scalar(bpn[:, 1024:2048], scr2[:, 0:1024],
                                        0.0, 256.0, ALU.min, ALU.add)
                t = n - 3
                if 0 <= t < NT:
                    wait(("x2", t))
                    i1 = O(v.tensor_reduce if live else None,
                           colsB[:, 2 + t % 2:3 + t % 2],
                           scr1[:, 1024 * (t % 2):1024 * (t % 2) + 1024],
                           axis=AX.X, op=ALU.max,
                           apply_absolute_value=True) if live else _NOP
                    inc(i1, "DV", 1, ("amax", t))
                t = n - 4
                if 0 <= t < NT:
                    if t >= 2:
                        wait(("aqT", t - 2))
                    fence(O(v.engine_nop if live else None)
                          if live else _NOP, "DV")
                    i1 = O(v.scalar_tensor_tensor if live else None,
                           aqbuf[:, 1024 * (t % 2):1024 * (t % 2) + 1024],
                           bpn[:, 0:1024], -384.0, bpn[:, 1024:2048],
                           ALU.add, ALU.add) if live else _NOP
                    inc(i1, "DV", 1, ("aq", t))

        progs = [prog_sync, prog_gpsimd, prog_tensor, prog_scalar,
                 prog_vector]

        C = _Ctx()
        for f in progs:
            C.reset_vals()
            f(None, C)
        C.planning = False

        for f, m in zip(progs, ["sync", "gpsimd", "tensor", "scalar",
                                "vector"]):
            C.reset_vals()
            getattr(blk, m)(lambda e, f=f: f(e, C))
    finally:
        for g in reversed(guards):
            try:
                g.__exit__(None, None, None)
            except Exception:
                pass

    return nc


_NC_CACHE = {}


def kernel(L0_attn_x, L0_attn_W_qkv, L0_ffn_weight):
    x = np.ascontiguousarray(np.asarray(L0_attn_x, dtype=np.float32))
    wq = np.ascontiguousarray(np.asarray(L0_attn_W_qkv, dtype=np.float32))
    wf = np.ascontiguousarray(np.asarray(L0_ffn_weight, dtype=np.float32))
    assert x.shape == (4, 4096, 1024)

    if "nc" not in _NC_CACHE:
        _NC_CACHE["nc"] = build_nc()
    nc = _NC_CACHE["nc"]

    in_maps = []
    for c in range(8):
        seq, half = c // 2, c % 2
        xs = np.ascontiguousarray(x[seq, half * TPC:(half + 1) * TPC, :])
        in_maps.append(dict(x=xs, wqkv=wq, wffn=wf))

    res = run_bass_kernel_spmd(nc, in_maps, core_ids=list(range(8)))
    out = np.empty((4, 4096, 1024), dtype=np.float32)
    for c in range(8):
        seq, half = c // 2, c % 2
        out[seq, half * TPC:(half + 1) * TPC, :] = res.results[c]["y"]
    return out


# version 24
# speedup vs baseline: 1.0406x; 1.0406x over previous
import numpy as np
import concourse.bass as bass
import concourse.mybir as mybir
from concourse.bass_utils import run_bass_kernel_spmd

F32 = mybir.dt.float32
BF16 = mybir.dt.bfloat16
ALU = mybir.AluOpType
ACT = mybir.ActivationFunctionType
AX = mybir.AxisListType

D = 1024
H = 16
TPC = 2048
NT = TPC // 128
WST = 12
EPS_LN = 1e-5
EPS_ATT = 1e-6
EPS_FFN = 1e-8
SCALE = 0.125


class _Nop:
    def then_inc(self, *a, **k):
        return self


_NOP = _Nop()


class _Ctx:
    def __init__(self):
        self.marks = {}
        self.planning = True
        self.vals = {}

    def reset_vals(self):
        self.vals = {}

    def helpers(self, eng, sems):
        planning = eng is None

        def wait(label):
            if not planning:
                sem, v = self.marks[label]
                eng.wait_ge(sems[sem], v)

        def inc(inst, sem, n, label=None):
            v = self.vals.get(sem, 0) + n
            self.vals[sem] = v
            if label is not None and self.planning:
                assert label not in self.marks, label
                self.marks[label] = (sem, v)
            if not planning:
                inst.then_inc(sems[sem], n)

        def fence(inst, sem):
            v = self.vals.get(sem, 0) + 1
            self.vals[sem] = v
            if not planning:
                inst.then_inc(sems[sem], 1)
                eng.wait_ge(sems[sem], v)

        return wait, inc, fence


def _ap(base_ap, extra_off, dims):
    p = list(base_ap.ap[0])
    return bass.AP(base_ap.tensor, base_ap.offset + extra_off,
                   [p] + [list(d) for d in dims])


def build_nc():
    nc = bass.Bass("TRN2", target_bir_lowering=False, debug=False)

    x_h = nc.dram_tensor("x", [TPC, D], F32, kind="ExternalInput").ap()
    wq_h = nc.dram_tensor("wqkv", [3 * D, D], F32, kind="ExternalInput").ap()
    wf_h = nc.dram_tensor("wffn", [D, D], F32, kind="ExternalInput").ap()
    y_h = nc.dram_tensor("y", [TPC, D], F32, kind="ExternalOutput").ap()
    cc_in = nc.dram_tensor("cc_in", [128, 1040], BF16, kind="Internal")
    cc_out = nc.dram_tensor("cc_out", [128, 1040], BF16, kind="Internal")

    guards = []

    def alloc(name, shape, dtype, side=None, td=False):
        g = nc.sbuf_tensor(name, shape, dtype, side=side,
                           is_transpose_dest=td)
        t = g.__enter__()
        guards.append(g)
        return t

    wqT = alloc("wqT", [128, 8192], BF16, "left", td=True)
    q_fm = alloc("q_fm", [128, H * 1024], BF16, "left", td=True)
    arena = alloc("arena", [128, 4096], BF16, "left")
    x_in = alloc("x_in", [128, 3072], F32, "left")
    x1T = alloc("x1T", [128, 3072], BF16, "left", td=True)
    scr1 = alloc("scr1", [128, 2048], F32, "left")
    scr2 = alloc("scr2", [128, 2048], F32, "left")
    stats = alloc("stats", [128, 16], F32, "left")
    stats2 = alloc("stats2", [128, 128], F32, "left")
    wcol = alloc("wcol", [128, 16], F32, "left")
    colsB = alloc("colsB", [128, 48], F32, "left")
    ones_sq = alloc("ones_sq", [128, 128], F32, "left")
    kv_sb = alloc("kv_sb", [128, 1040], F32, "left")
    kv_bd = alloc("kv_bd", [128, 1040], BF16, "left")
    aqbuf = alloc("aqbuf", [128, 2048], BF16, "left")

    gA2 = []

    def allocA2(name, shape, dtype, td=False):
        g = nc.sbuf_tensor(name, shape, dtype, side="left",
                           is_transpose_dest=td)
        t = g.__enter__()
        gA2.append(g)
        return t

    qc = allocA2("qc", [128, 4096], BF16)
    sq_q = allocA2("sq_q", [128, 2048], BF16)
    sq_k = allocA2("sq_k", [128, 2048], BF16)
    qn = allocA2("qn", [128, 2048], BF16)
    kn = allocA2("kn", [128, 2048], BF16)
    v_ext = allocA2("v_ext", [128, 4160], BF16)
    x1 = allocA2("x1", [128, 2048], BF16)
    WfA = allocA2("WfA", [128, 2048], F32)

    gA1 = nc.sbuf_tensor("W_T", [128, 24576], BF16, side="right",
                         is_transpose_dest=True)
    W_T = gA1.__enter__()

    gA1.__exit__(None, None, None)
    Wf2 = alloc("Wf2", [128, 8192], F32, "right")
    attn_s = alloc("attn_s", [128, 2048], F32, "right")
    y_out = alloc("y_out", [128, 2048], F32, "right")

    for g in reversed(gA2):
        g.__exit__(None, None, None)
    x_mid = alloc("x_mid", [128, 7168], F32, "left")

    bpn = arena

    try:
        P_g = nc.psum_tensor("P", [128, 4096], F32)
        P = P_g.__enter__()
        guards.append(P_g)
        Pap = P[:]

        sem_names = ["SX", "SG", "PE", "DV", "AC", "GP", "CC", "ST"]
        sem_gs = {n: nc.semaphore(n) for n in sem_names}
        sems = {n: g.__enter__() for n, g in sem_gs.items()}
        guards.extend(sem_gs.values())
        blk_g = nc.Block()
        blk = blk_g.__enter__()
        guards.append(blk_g)

        KVBASE = [2560 + 512 * (g // 3) + 160 * (g % 3) for g in range(8)]


        def prog_sync(eng, C):
            wait, inc, fence = C.helpers(eng, sems)
            live = eng is not None

            def DMA(label, out, in_, sem="SX", **kw):
                i = eng.dma_start(out=out, in_=in_, **kw) if live else _NOP
                inc(i, sem, 16, label)

            wait(("vinit",))

            def do_xld(n, labels=("xld",)):
                if n >= 3:
                    wait(("ln1", n - 3))
                DMA((labels[0], n),
                    x_in[:, 1024 * (n % 3):1024 * (n % 3) + 1024],
                    x_h[128 * n:128 * (n + 1), :])

            def do_x1T(t):
                wait(("ln1", t))
                if t >= 6:
                    wait(("vmm", (t - 6, 1)))
                x1_3 = x1[:, 1024 * (t % 2):1024 * (t % 2) + 1024] \
                    .rearrange("p (k f) -> p k f", k=8)
                x1T_3 = x1T[:, 1024 * (t % 3):1024 * (t % 3) + 1024] \
                    .rearrange("p (k f) -> p k f", k=8)
                DMA(("x1T", t), x1T_3, x1_3, transpose=True)

            def do_wT(s):
                wait(("wload", s))
                for oj in range(2):
                    a_in = arena[:, 2048 * (s % 2) + 1024 * oj:
                                 2048 * (s % 2) + 1024 * (oj + 1)] \
                        .rearrange("p (cc c) -> p cc c", cc=8)
                    o_out = _ap(W_T[:], 256 * s + 128 * oj,
                                [[3072, 8], [1, 128]]) if live else None
                    i = eng.dma_start(out=o_out, in_=a_in,
                                      transpose=True) if live else _NOP
                    inc(i, "SX", 16, ("wT", s) if oj == 1 else None)

            do_xld(0)
            do_xld(1)
            do_xld(2)
            do_wT(0)
            do_wT(1)
            do_xld(3)
            do_x1T(0)
            do_wT(2)
            do_wT(3)
            do_xld(4)
            do_x1T(1)
            do_wT(4)
            do_wT(5)
            do_xld(5)
            do_x1T(2)
            for s in range(6, WST):
                do_wT(s)

            for n in range(6, NT + 8):
                if n < NT:
                    do_xld(n)
                t = n - 3
                if 0 <= t < NT:
                    do_x1T(t)
                t = n - 8
                if 0 <= t < NT:
                    wait(("qn", t))
                    qn_3 = qn[:, 1024 * (t % 2):1024 * (t % 2) + 1024] \
                        .rearrange("p (k f) -> p k f", k=8)
                    qfm_3 = _ap(q_fm[:], 128 * t,
                                [[2048, 8], [1, 128]]) if live else None
                    i2 = eng.dma_start(out=qfm_3, in_=qn_3, transpose=True) \
                        if live else _NOP
                    inc(i2, "SX", 16, ("qT", t))

            for n in range(3):
                wait(("ln1", [15, 13, 14][n]))
                DMA(("xldB", n),
                    x_in[:, 1024 * (n % 3):1024 * (n % 3) + 1024],
                    x_h[128 * n:128 * (n + 1), :])

            for h2 in range(2):
                wait(("wq", 4 * h2 + 3))
                for oc in range(4):
                    a_in = arena[:, 1024 * oc:1024 * (oc + 1)].rearrange(
                        "p (ii c) -> p ii c", ii=8)
                    o_out = _ap(wqT[:], 512 * h2 + 128 * oc,
                                [[1024, 8], [1, 128]]) if live else None
                    i = eng.dma_start(out=o_out, in_=a_in, transpose=True) \
                        if live else _NOP
                    inc(i, "SX", 16, ("wqT", h2) if oc == 3 else None)

            for n in range(3, NT + 8):
                if n < NT:
                    wait(("xmid", n - 3))
                    DMA(("xldB", n),
                        x_in[:, 1024 * (n % 3):1024 * (n % 3) + 1024],
                        x_h[128 * n:128 * (n + 1), :])
                t = n - 5
                if 0 <= t < NT:
                    wait(("aq", t))
                    if t >= 3:
                        wait(("ffnmm", t - 3))
                    else:
                        wait(("vmm", ([15, 13, 14][t], 1)))
                    aq_3 = aqbuf[:, 1024 * (t % 2):1024 * (t % 2) + 1024] \
                        .rearrange("p (k f) -> p k f", k=8)
                    aqT_3 = x1T[:, 1024 * (t % 3):1024 * (t % 3) + 1024] \
                        .rearrange("p (k f) -> p k f", k=8)
                    DMA(("aqT", t), aqT_3, aq_3, transpose=True)
                t = n - 8
                if 0 <= t < NT:
                    wait(("yout", t))
                    DMA(("yst", t), y_h[128 * t:128 * (t + 1), :],
                        y_out[:, 1024 * (t % 2):1024 * (t % 2) + 1024],
                        sem="ST")
            if live:
                eng.wait_ge(sems["ST"], 16 * NT)

        def prog_gpsimd(eng, C):
            wait, inc, fence = C.helpers(eng, sems)
            live = eng is not None

            for s in range(WST):
                if s >= 2:
                    wait(("wT", s - 2))
                src = wq_h[256 * s:256 * (s + 1), :].rearrange(
                    "(oj p) c -> p oj c", p=128)
                dst = arena[:, 2048 * (s % 2):2048 * (s % 2 + 1)].rearrange(
                    "p (oj c) -> p oj c", oj=2)
                i = eng.dma_start(out=dst, in_=src) if live else _NOP
                inc(i, "SG", 16, ("wload", s))

            for j in range(8):
                if j >= 2:
                    wait(("wsum", j - 2))
                i = eng.dma_start(out=WfA[:, 1024 * (j % 2):
                                          1024 * (j % 2) + 1024],
                                  in_=wf_h[128 * j:128 * (j + 1), :]) \
                    if live else _NOP
                inc(i, "SG", 16, ("wfld", j))

            wait(("qkmm", NT - 1))
            wait(("vmm", (NT - 1, 0)))
            wait(("vmm", (NT - 1, 1)))
            for j in range(8):
                i = eng.dma_start(out=Wf2[:, 1024 * j:1024 * (j + 1)],
                                  in_=wf_h[128 * j:128 * (j + 1), :]) \
                    if live else _NOP
                inc(i, "SG", 16, ("wfld2", j))

            wait(("qT", NT - 1))
            wait(("kvacc", NT - 1))
            i = eng.dma_start(out=cc_in.ap(), in_=kv_sb[:]) if live else _NOP
            inc(i, "SG", 16, ("ccin",))
            wait(("ccin",))
            if live:
                cc = eng.collective_compute(
                    "AllReduce", ALU.add,
                    replica_groups=[[0, 1], [2, 3], [4, 5], [6, 7]],
                    ins=[cc_in.ap().opt()],
                    outs=[cc_out.ap().opt()],
                )
                cc.then_inc(sems["CC"], 1)
                eng.wait_ge(sems["CC"], 1)
            i = eng.dma_start(out=kv_sb[:], in_=cc_out.ap()) if live else _NOP
            inc(i, "SG", 16, ("kvtot",))

            for n in range(1, NT + 5):
                t = n - 1
                if 0 <= t < NT:
                    wait(("attn_sc", t))
                    wait(("xldB", t))
                    if t >= 7:
                        wait(("yout", t - 7))
                    i = eng.tensor_add(x_mid[:, 1024 * (t % 7):
                                             1024 * (t % 7) + 1024],
                                       attn_s[:, 1024 * (t % 2):
                                              1024 * (t % 2) + 1024],
                                       x_in[:, 1024 * (t % 3):
                                            1024 * (t % 3) + 1024]) \
                        if live else _NOP
                    inc(i, "GP", 1, ("xmid", t))


        def prog_tensor(eng, C):
            wait, inc, fence = C.helpers(eng, sems)
            live = eng is not None

            def MM(out, lhsT, rhs, start, stop):
                return eng.matmul(out, lhsT, rhs, start=start, stop=stop) \
                    if live else _NOP

            for n in range(NT + 8):
                t = n - 5
                if 0 <= t < NT:
                    wait(("vextA", t))
                    wait(("x1T", t))
                    wait(("wT", 11))
                    last = _NOP
                    for ci in range(8):
                        last = MM(Pap[:, 2048:2560],
                                  x1T[:, 1024 * (t % 3) + 128 * ci:]
                                  [:, 0:128],
                                  W_T[:, 3072 * ci + 2560:][:, 0:512],
                                  ci == 0, ci == 7)
                    inc(last, "PE", 1, ("vmm", (t, 1)))
                t = n - 8
                if 0 <= t < NT:
                    wait(("kn", t))
                    wait(("vextB", t))
                    if t >= 1:
                        wait(("kvacc", t - 1))
                    last = _NOP
                    for g in range(8):
                        last = MM(Pap[:, KVBASE[g]:KVBASE[g] + 129],
                                  kn[:, 1024 * (t % 2) + 128 * g:][:, 0:128],
                                  v_ext[:, 1040 * (t % 4) + 130 * g:]
                                  [:, 0:129],
                                  True, True)
                    inc(last, "PE", 1, ("kvmm", t))
                t = n - 4
                if 0 <= t < NT:
                    wait(("x1T", t))
                    wait(("wT", 9))
                    if t >= 1:
                        wait(("vextB", t - 1))
                    last = _NOP
                    for ci in range(8):
                        last = MM(Pap[:, 2048:2560],
                                  x1T[:, 1024 * (t % 3) + 128 * ci:]
                                  [:, 0:128],
                                  W_T[:, 3072 * ci + 2048:][:, 0:512],
                                  ci == 0, ci == 7)
                    inc(last, "PE", 1, ("vmm", (t, 0)))
                    if t >= 1:
                        wait(("evq", t - 1))
                        wait(("evk", t - 1))
                    last = _NOP
                    if t < 2:
                        for jj in range(4):
                            wait(("wT", 2 * jj + 1))
                            for ci in range(8):
                                last = MM(Pap[:, 512 * jj:512 * (jj + 1)],
                                          x1T[:, 1024 * (t % 3) + 128 * ci:]
                                          [:, 0:128],
                                          W_T[:, 3072 * ci + 512 * jj:]
                                          [:, 0:512],
                                          ci == 0, ci == 7)
                    else:
                        wait(("wT", 7))
                        for ci in range(8):
                            for jj in range(4):
                                last = MM(Pap[:, 512 * jj:512 * (jj + 1)],
                                          x1T[:, 1024 * (t % 3) + 128 * ci:]
                                          [:, 0:128],
                                          W_T[:, 3072 * ci + 512 * jj:]
                                          [:, 0:512],
                                          ci == 0, ci == 7)
                    inc(last, "PE", 1, ("qkmm", t))

            wait(("wsums",))
            wait(("kvacc", NT - 1))
            m = MM(Pap[:, 4000:4001], ones_sq[:], wcol[:, 0:1], True, True)
            inc(m, "PE", 1, ("wsmm",))

            for n in range(NT + 6):
                t = n - 6
                if 0 <= t < NT:
                    wait(("aqT", t))
                    if t == 0:
                        wait(("wqT", 0))
                        wait(("wqT", 1))
                    if t >= 2:
                        wait(("yout", t - 2))
                    last = _NOP
                    for jj in range(2):
                        for ci in range(8):
                            last = MM(Pap[:, 2048 + 1024 * (t % 2) +
                                          512 * jj:][:, 0:512],
                                      x1T[:, 1024 * (t % 3) + 128 * ci:]
                                      [:, 0:128],
                                      wqT[:, 1024 * ci + 512 * jj:]
                                      [:, 0:512],
                                      ci == 0, ci == 7)
                    inc(last, "PE", 1, ("ffnmm", t))
                t = n
                if t < NT:
                    if t == 0:
                        wait(("kvbd",))
                    wait(("qT", t))
                    if t >= 1:
                        wait(("attn_sc", t - 1))
                    last = _NOP
                    for g in range(8):
                        last = MM(Pap[:, 256 * g:256 * g + 130],
                                  q_fm[:, 2048 * g + 128 * t:][:, 0:128],
                                  kv_bd[:, 130 * g:130 * (g + 1)],
                                  True, True)
                    inc(last, "PE", 1, ("outmm", t))

        def prog_scalar(eng, C):
            wait, inc, fence = C.helpers(eng, sems)
            live = eng is not None

            def A(out, in_, func, **kw):
                return eng.activation(out, in_, func, **kw) if live else _NOP

            for n in range(NT + 7):
                t = n - 5
                if 0 <= t < NT:
                    wait(("qkmm", t))
                    if t >= 2:
                        wait(("kln", t - 2))
                    a = A(qc[:, 2048 * (t % 2) + 1024:2048 * (t % 2) + 2048],
                          Pap[:, 1024:2048], ACT.Copy)
                    inc(a, "AC", 1, ("evk", t))
                t = n - 2
                if 0 <= t < NT:
                    wait(("var", t))
                    A(stats[:, 8 * (t % 2) + 6:8 * (t % 2) + 7],
                      stats[:, 8 * (t % 2) + 5:8 * (t % 2) + 6],
                      ACT.Ln, bias=wcol[:, 13:14], scale=wcol[:, 1:2])
                t = n - 7
                if 0 <= t < NT:
                    wait(("qln", t))
                    A(scr1[:, 1024 * (t % 2):1024 * (t % 2) + 1024],
                      scr1[:, 1024 * (t % 2):1024 * (t % 2) + 1024], ACT.Exp)
                    wait(("kln", t))
                    A(scr2[:, 1024 * (t % 2):1024 * (t % 2) + 1024],
                      scr2[:, 1024 * (t % 2):1024 * (t % 2) + 1024], ACT.Exp)
                t = n - 2
                if 0 <= t < NT:
                    a = A(stats[:, 8 * (t % 2) + 7:8 * (t % 2) + 8],
                          stats[:, 8 * (t % 2) + 6:8 * (t % 2) + 7],
                          ACT.Exp, scale=wcol[:, 15:16])
                    inc(a, "AC", 1, ("inv1", t))
                t = n - 1
                if 0 <= t < NT:
                    wait(("xld", t))
                    if t >= 2:
                        wait(("var", t - 2))
                    a = A(aqbuf[:, 0:1024],
                          x_in[:, 1024 * (t % 3):1024 * (t % 3) + 1024],
                          ACT.Square,
                          accum_out=stats[:, 8 * (t % 2) + 1:
                                          8 * (t % 2) + 2])
                    inc(a, "AC", 1, ("s2", t))
                t = n - 6
                if 0 <= t < NT:
                    wait(("ssk", t))
                    if t >= 2:
                        wait(("kln", t - 2))
                    s2B = 64 * (t % 2)
                    A(stats2[:, s2B + 16:s2B + 32], stats2[:, s2B:s2B + 16],
                      ACT.Ln)
                    A(stats2[:, s2B + 48:s2B + 64],
                      stats2[:, s2B + 32:s2B + 48], ACT.Ln)
                t = n - 7
                if 0 <= t < NT:
                    if t >= 2:
                        wait(("qT", t - 2))
                    a = A(qn[:, 1024 * (t % 2):1024 * (t % 2) + 1024],
                          scr1[:, 1024 * (t % 2):1024 * (t % 2) + 1024],
                          ACT.Ln, bias=wcol[:, 14:15])
                    inc(a, "AC", 1, ("qn", t))
                    if t >= 2:
                        wait(("kvmm", t - 2))
                    a = A(kn[:, 1024 * (t % 2):1024 * (t % 2) + 1024],
                          scr2[:, 1024 * (t % 2):1024 * (t % 2) + 1024],
                          ACT.Ln, bias=wcol[:, 14:15])
                    inc(a, "AC", 1, ("kn", t))
                t = n - 6
                if 0 <= t < NT:
                    s2B = 64 * (t % 2)
                    A(stats2[:, s2B + 16:s2B + 32],
                      stats2[:, s2B + 16:s2B + 32], ACT.Exp,
                      scale=wcol[:, 15:16])
                    a = A(stats2[:, s2B + 48:s2B + 64],
                          stats2[:, s2B + 48:s2B + 64], ACT.Exp,
                          scale=wcol[:, 15:16])
                    inc(a, "AC", 1, ("invqk", t))
                t = n - 5
                if 0 <= t < NT:
                    wait(("evq", t))
                    if t >= 2:
                        wait(("ssk", t - 2))
                    a = A(sq_q[:, 1024 * (t % 2):1024 * (t % 2) + 1024],
                          qc[:, 2048 * (t % 2):2048 * (t % 2) + 1024],
                          ACT.Square)
                    inc(a, "AC", 1, ("sqq", t))
                    a = A(sq_k[:, 1024 * (t % 2):1024 * (t % 2) + 1024],
                          qc[:, 2048 * (t % 2) + 1024:2048 * (t % 2) + 2048],
                          ACT.Square)
                    inc(a, "AC", 1, ("sqk", t))

            for n in range(NT + 4):
                t = n - 3
                if 0 <= t < NT:
                    wait(("varB", t))
                    A(colsB[:, 30 + t % 2:31 + t % 2],
                      colsB[:, 28 + t % 2:29 + t % 2],
                      ACT.Ln, bias=wcol[:, 13:14], scale=wcol[:, 1:2])
                t = n - 2
                if 0 <= t < NT:
                    wait(("xmid", t))
                    if t >= 1:
                        wait(("varB", t - 1))
                    A(attn_s[:, 1024 * (t % 2):1024 * (t % 2) + 1024],
                      x_mid[:, 1024 * (t % 7):1024 * (t % 7) + 1024],
                      ACT.Copy, accum_out=colsB[:, 0:1])
                    a = A(attn_s[:, 1024 * (t % 2):1024 * (t % 2) + 1024],
                          x_mid[:, 1024 * (t % 7):1024 * (t % 7) + 1024],
                          ACT.Square, accum_out=colsB[:, 1:2])
                    inc(a, "AC", 1, ("s2B", t))
                t = n - 3
                if 0 <= t < NT:
                    fence(_NOP if not live else
                          eng.activation(colsB[:, 38:39], wcol[:, 14:15],
                                         ACT.Copy), "AC")
                    a = A(colsB[:, 6 + t % 2:7 + t % 2],
                          colsB[:, 30 + t % 2:31 + t % 2],
                          ACT.Exp, scale=wcol[:, 15:16])
                    inc(a, "AC", 1, ("inv2", t))
                    if t >= 2:
                        wait(("z", t - 2))
                    a = A(scr1[:, 1024 * (t % 2):1024 * (t % 2) + 1024],
                          x_mid[:, 1024 * (t % 7):1024 * (t % 7) + 1024],
                          ACT.Identity, bias=colsB[:, 4 + t % 2:5 + t % 2])
                    fence(a, "AC")
                    a = A(scr1[:, 1024 * (t % 2):1024 * (t % 2) + 1024],
                          scr1[:, 1024 * (t % 2):1024 * (t % 2) + 1024],
                          ACT.Copy, scale=colsB[:, 6 + t % 2:7 + t % 2])
                    inc(a, "AC", 1, ("x2", t))

        def prog_vector(eng, C):
            wait, inc, fence = C.helpers(eng, sems)
            live = eng is not None

            def O(fn, *a, **kw):
                return fn(*a, **kw) if live else _NOP

            v = eng
            if live:
                v.memset(wcol[:, 13:14], EPS_LN)
                v.memset(wcol[:, 1:2], -1.0)
                v.memset(wcol[:, 14:15], 1.0)
                v.memset(wcol[:, 15:16], -0.5)
                v.memset(ones_sq[:], 1.0)
                v.memset(kv_sb[:], 0.0)
                iz = v.memset(kv_bd[:], 0.0)
                fence(iz, "DV")
                ve3 = v_ext[:].rearrange("p (s g c) -> p s g c", s=4, g=8)
                i1 = v.memset(ve3[:, :, :, 128:129], 1.0)
            else:
                fence(_NOP, "DV")
                i1 = _NOP
            inc(i1, "DV", 1, ("vinit",))

            for n in range(NT + 10):
                t = n - 9
                if 0 <= t < NT:
                    wait(("kvmm", t))
                    if live:
                        for (poff, koff, ng) in ((2560, 0, 3), (3072, 390, 3),
                                                 (3584, 780, 2)):
                            pin = _ap(Pap, poff, [[160, ng], [1, 129]])
                            acc = kv_sb[:, koff:koff + 130 * ng].rearrange(
                                "p (g c) -> p g c", g=ng)[:, :, 0:129]
                            i1 = v.scalar_tensor_tensor(acc, pin, 1.0, acc,
                                                        ALU.mult, ALU.add)
                    else:
                        i1 = _NOP
                    inc(i1, "DV", 1, ("kvacc", t))
                t = n - 5
                if 0 <= t < NT:
                    wait(("qkmm", t))
                    if t >= 2:
                        wait(("sqk", t - 2))
                    i1 = O(v.tensor_copy if live else None,
                           qc[:, 2048 * (t % 2):2048 * (t % 2) + 1024],
                           Pap[:, 0:1024]) if live else _NOP
                    inc(i1, "DV", 1, ("evq", t))
                    wait(("vmm", (t, 1)))
                    if live:
                        src = Pap[:, 2048:2560].rearrange(
                            "p (g d) -> p g d", g=4)
                        dst = _ap(v_ext[:], 1040 * (t % 4) + 520,
                                  [[130, 4], [1, 128]])
                        i1 = v.tensor_copy(dst, src)
                    else:
                        i1 = _NOP
                    inc(i1, "DV", 1, ("vextB", t))
                t = n - 6
                if 0 <= t < NT:
                    s2B = 64 * (t % 2)
                    wait(("sqq", t))
                    O(v.reduce_sum if live else None,
                      stats2[:, s2B:s2B + 16],
                      sq_q[:, 1024 * (t % 2):1024 * (t % 2) + 1024]
                      .rearrange("p (h d) -> p h d", h=16), axis=AX.X)
                    wait(("sqk", t))
                    i1 = O(v.reduce_sum if live else None,
                           stats2[:, s2B + 32:s2B + 48],
                           sq_k[:, 1024 * (t % 2):1024 * (t % 2) + 1024]
                           .rearrange("p (h d) -> p h d", h=16), axis=AX.X) \
                        if live else _NOP
                    inc(i1, "DV", 1, ("ssk", t))
                t = n - 1
                if 0 <= t < NT:
                    wait(("xld", t))
                    i1 = O(v.reduce_sum if live else None,
                           stats[:, 8 * (t % 2):8 * (t % 2) + 1],
                           x_in[:, 1024 * (t % 3):1024 * (t % 3) + 1024],
                           axis=AX.X) if live else _NOP
                    fence(i1, "DV")
                j = n - 6
                if 0 <= j < 8:
                    wait(("wfld", j))
                    i1 = O(v.reduce_sum if live else None,
                           wcol[:, 2 + j:3 + j],
                           WfA[:, 1024 * (j % 2):1024 * (j % 2) + 1024],
                           axis=AX.X, apply_absolute_value=True) \
                        if live else _NOP
                    inc(i1, "DV", 1, ("wsum", j))
                t = n - 1
                if 0 <= t < NT:
                    wait(("s2", t))
                    if live:
                        v.tensor_scalar(stats[:, 8 * (t % 2) + 3:
                                              8 * (t % 2) + 4],
                                        stats[:, 8 * (t % 2) + 1:
                                              8 * (t % 2) + 2],
                                        1.0 / D, None, ALU.mult)
                        v.tensor_scalar(stats[:, 8 * (t % 2) + 2:
                                              8 * (t % 2) + 3],
                                        stats[:, 8 * (t % 2):
                                              8 * (t % 2) + 1],
                                        1.0 / D, None, ALU.mult)
                t = n - 6
                if 0 <= t < NT:
                    wait(("invqk", t))
                    s2B = 64 * (t % 2)
                    i1 = O(v.tensor_tensor if live else None,
                           scr1[:, 1024 * (t % 2):1024 * (t % 2) + 1024]
                           .rearrange("p (h d) -> p h d", h=16),
                           qc[:, 2048 * (t % 2):2048 * (t % 2) + 1024]
                           .rearrange("p (h d) -> p h d", h=16),
                           stats2[:, s2B + 16:s2B + 32].unsqueeze(2)
                           .broadcast_to([128, 16, 64]),
                           ALU.mult) if live else _NOP
                    inc(i1, "DV", 1, ("qln", t))
                    i1 = O(v.tensor_tensor if live else None,
                           scr2[:, 1024 * (t % 2):1024 * (t % 2) + 1024]
                           .rearrange("p (h d) -> p h d", h=16),
                           qc[:, 2048 * (t % 2) + 1024:
                              2048 * (t % 2) + 2048]
                           .rearrange("p (h d) -> p h d", h=16),
                           stats2[:, s2B + 48:s2B + 64].unsqueeze(2)
                           .broadcast_to([128, 16, 64]),
                           ALU.mult) if live else _NOP
                    inc(i1, "DV", 1, ("kln", t))
                t = n - 2
                if 0 <= t < NT:
                    wait(("inv1", t))
                    if t >= 2:
                        wait(("x1T", t - 2))
                    i1 = O(v.tensor_scalar if live else None,
                           x1[:, 1024 * (t % 2):1024 * (t % 2) + 1024],
                           x_in[:, 1024 * (t % 3):1024 * (t % 3) + 1024],
                           stats[:, 8 * (t % 2) + 2:8 * (t % 2) + 3],
                           stats[:, 8 * (t % 2) + 7:8 * (t % 2) + 8],
                           ALU.subtract, ALU.mult) if live else _NOP
                    inc(i1, "DV", 1, ("ln1", t))
                t = n - 1
                if 0 <= t < NT:
                    fence(O(v.engine_nop if live else None)
                          if live else _NOP, "DV")
                    i1 = O(v.scalar_tensor_tensor if live else None,
                           stats[:, 8 * (t % 2) + 5:8 * (t % 2) + 6],
                           stats[:, 8 * (t % 2) + 2:8 * (t % 2) + 3],
                           stats[:, 8 * (t % 2) + 2:8 * (t % 2) + 3],
                           stats[:, 8 * (t % 2) + 3:8 * (t % 2) + 4],
                           ALU.mult, ALU.subtract) if live else _NOP
                    inc(i1, "DV", 1, ("var", t))
                t = n - 4
                if 0 <= t < NT:
                    wait(("vmm", (t, 0)))
                    if t >= 4:
                        wait(("kvmm", t - 4))
                    if live:
                        src = Pap[:, 2048:2560].rearrange(
                            "p (g d) -> p g d", g=4)
                        dst = _ap(v_ext[:], 1040 * (t % 4),
                                  [[130, 4], [1, 128]])
                        i1 = v.tensor_copy(dst, src)
                    else:
                        i1 = _NOP
                    inc(i1, "DV", 1, ("vextA", t))

            i1 = O(v.reduce_sum if live else None, wcol[:, 0:1],
                   wcol[:, 2:10], axis=AX.X) if live else _NOP
            inc(i1, "DV", 1, ("wsums",))
            wait(("wsmm",))
            if live:
                iw = v.tensor_scalar(wcol[:, 10:11], Pap[:, 4000:4001],
                                     1.0 / (1024.0 * 1024.0), EPS_FFN,
                                     ALU.mult, ALU.add)
                fence(iw, "DV")
                v.tensor_scalar(wcol[:, 11:12], wcol[:, 10:11], 0.5, None,
                                ALU.mult)
                iw = v.tensor_scalar(wcol[:, 12:13], wcol[:, 10:11], -0.5,
                                     None, ALU.mult)
                fence(iw, "DV")
            else:
                fence(_NOP, "DV")
                fence(_NOP, "DV")
            for j in range(4):
                wait(("wfld2", j))
                if live:
                    v.tensor_scalar(scr1[:, 0:1024],
                                    Wf2[:, 1024 * j:1024 * (j + 1)],
                                    wcol[:, 11:12], 0.0,
                                    ALU.subtract, ALU.is_gt)
                    v.tensor_scalar(scr2[:, 0:1024],
                                    Wf2[:, 1024 * j:1024 * (j + 1)],
                                    wcol[:, 12:13], 0.0,
                                    ALU.subtract, ALU.is_lt)
                    i1 = v.tensor_sub(
                        arena[:, 1024 * j:1024 * (j + 1)],
                        scr1[:, 0:1024], scr2[:, 0:1024])
                else:
                    i1 = _NOP
                inc(i1, "DV", 1, ("wq", j))
            wait(("kvtot",))
            if live:
                bd = kv_bd[:].rearrange("p (g c) -> p g c", g=8)
                sbk = kv_sb[:].rearrange("p (g c) -> p g c", g=8)
                v.tensor_scalar(bd[0:64, :, 0:64], sbk[0:64, :, 0:64],
                                SCALE, None, ALU.mult)
                v.tensor_scalar(bd[64:128, :, 64:128],
                                sbk[64:128, :, 64:128], SCALE, None,
                                ALU.mult)
                v.tensor_scalar(bd[0:64, :, 128:129], sbk[0:64, :, 128:129],
                                EPS_ATT, None, ALU.add)
                i1 = v.tensor_scalar(bd[64:128, :, 129:130],
                                     sbk[64:128, :, 128:129],
                                     EPS_ATT, None, ALU.add)
            else:
                i1 = _NOP
            inc(i1, "DV", 1, ("kvbd",))
            for j in range(4, 8):
                wait(("wfld2", j))
                if j >= 4:
                    wait(("wqT", 0))
                if live:
                    v.tensor_scalar(scr1[:, 0:1024],
                                    Wf2[:, 1024 * j:1024 * (j + 1)],
                                    wcol[:, 11:12], 0.0,
                                    ALU.subtract, ALU.is_gt)
                    v.tensor_scalar(scr2[:, 0:1024],
                                    Wf2[:, 1024 * j:1024 * (j + 1)],
                                    wcol[:, 12:13], 0.0,
                                    ALU.subtract, ALU.is_lt)
                    i1 = v.tensor_sub(
                        arena[:, 1024 * (j % 4):1024 * (j % 4 + 1)],
                        scr1[:, 0:1024], scr2[:, 0:1024])
                else:
                    i1 = _NOP
                inc(i1, "DV", 1, ("wq", j))

            for n in range(NT + 8):
                t = n - 1
                if 0 <= t < NT:
                    wait(("outmm", t))
                    if live:
                        ir = v.reciprocal(colsB[:, 8:24],
                                          _ap(Pap, 128, [[256, 8], [1, 2]]))
                        fence(ir, "DV")
                    else:
                        fence(_NOP, "DV")
                t = n - 1
                if 0 <= t < NT:
                    if t >= 2:
                        wait(("s2B", t - 2))
                    i1 = O(v.tensor_tensor if live else None,
                           attn_s[:, 1024 * (t % 2):1024 * (t % 2) + 1024]
                           .rearrange("p (g w d) -> p g w d", g=8, w=2),
                           _ap(Pap, 0, [[256, 8], [64, 2], [1, 64]]),
                           colsB[:, 8:24].rearrange("p (g w) -> p g w", g=8)
                           .unsqueeze(3).broadcast_to([128, 8, 2, 64]),
                           ALU.mult) if live else _NOP
                    inc(i1, "DV", 1, ("attn_sc", t))
                t = n - 7
                if 0 <= t < NT:
                    wait(("ffnmm", t))
                    if t >= 2:
                        wait(("yst", t - 2))
                    i1 = O(v.scalar_tensor_tensor if live else None,
                           y_out[:, 1024 * (t % 2):1024 * (t % 2) + 1024],
                           Pap[:, 2048 + 1024 * (t % 2):
                               2048 + 1024 * (t % 2) + 1024],
                           colsB[:, 24 + t % 4:25 + t % 4],
                           x_mid[:, 1024 * (t % 7):1024 * (t % 7) + 1024],
                           ALU.mult, ALU.add) if live else _NOP
                    inc(i1, "DV", 1, ("yout", t))
                t = n - 4
                if 0 <= t < NT:
                    O(v.tensor_scalar if live else None,
                      colsB[:, 32 + t % 2:33 + t % 2],
                      colsB[:, 2 + t % 2:3 + t % 2],
                      EPS_FFN, None, ALU.add)
                t = n - 2
                if 0 <= t < NT:
                    wait(("s2B", t))
                    O(v.tensor_scalar if live else None,
                      colsB[:, 4 + t % 2:5 + t % 2], colsB[:, 0:1],
                      -1.0 / D, None, ALU.mult)
                    i1 = O(v.tensor_scalar if live else None,
                           colsB[:, 36:37], colsB[:, 1:2],
                           1.0 / D, None, ALU.mult) if live else _NOP
                    fence(i1, "DV")
                t = n - 4
                if 0 <= t < NT:
                    if not 0 <= n - 2 < NT:
                        fence(O(v.engine_nop if live else None)
                              if live else _NOP, "DV")
                    O(v.reciprocal if live else None,
                      colsB[:, 34 + t % 2:35 + t % 2],
                      colsB[:, 32 + t % 2:33 + t % 2])
                    i1 = O(v.tensor_scalar if live else None,
                           colsB[:, 24 + t % 4:25 + t % 4],
                           colsB[:, 32 + t % 2:33 + t % 2], wcol[:, 10:11],
                           1.0 / 127.0, ALU.mult, ALU.mult) if live else _NOP
                    fence(i1, "DV")
                t = n - 2
                if 0 <= t < NT:
                    i1 = O(v.scalar_tensor_tensor if live else None,
                           colsB[:, 28 + t % 2:29 + t % 2],
                           colsB[:, 4 + t % 2:5 + t % 2],
                           colsB[:, 4 + t % 2:5 + t % 2],
                           colsB[:, 36:37],
                           ALU.mult, ALU.subtract) if live else _NOP
                    inc(i1, "DV", 1, ("varB", t))
                t = n - 4
                if 0 <= t < NT:
                    if not 0 <= n - 2 < NT:
                        fence(O(v.engine_nop if live else None)
                              if live else _NOP, "DV")
                    i1 = O(v.tensor_scalar if live else None,
                           scr2[:, 0:1024],
                           scr1[:, 1024 * (t % 2):1024 * (t % 2) + 1024],
                           colsB[:, 34 + t % 2:35 + t % 2],
                           127.0, ALU.mult, ALU.mult) if live else _NOP
                    fence(i1, "DV")
                    inc(_NOP if not live else v.engine_nop(), "DV", 1,
                        ("z", t))
                    if t == 0:
                        wait(("wqT", 1))
                    if live:
                        v.tensor_scalar(bpn[:, 0:1024], scr2[:, 0:1024],
                                        0.0, 128.0, ALU.max, ALU.add)
                        v.tensor_scalar(bpn[:, 1024:2048], scr2[:, 0:1024],
                                        0.0, 256.0, ALU.min, ALU.add)
                t = n - 3
                if 0 <= t < NT:
                    wait(("x2", t))
                    i1 = O(v.tensor_reduce if live else None,
                           colsB[:, 2 + t % 2:3 + t % 2],
                           scr1[:, 1024 * (t % 2):1024 * (t % 2) + 1024],
                           axis=AX.X, op=ALU.max,
                           apply_absolute_value=True) if live else _NOP
                    inc(i1, "DV", 1, ("amax", t))
                t = n - 4
                if 0 <= t < NT:
                    if t >= 2:
                        wait(("aqT", t - 2))
                    fence(O(v.engine_nop if live else None)
                          if live else _NOP, "DV")
                    i1 = O(v.scalar_tensor_tensor if live else None,
                           aqbuf[:, 1024 * (t % 2):1024 * (t % 2) + 1024],
                           bpn[:, 0:1024], -384.0, bpn[:, 1024:2048],
                           ALU.add, ALU.add) if live else _NOP
                    inc(i1, "DV", 1, ("aq", t))

        progs = [prog_sync, prog_gpsimd, prog_tensor, prog_scalar,
                 prog_vector]

        C = _Ctx()
        for f in progs:
            C.reset_vals()
            f(None, C)
        C.planning = False

        for f, m in zip(progs, ["sync", "gpsimd", "tensor", "scalar",
                                "vector"]):
            C.reset_vals()
            getattr(blk, m)(lambda e, f=f: f(e, C))
    finally:
        for g in reversed(guards):
            try:
                g.__exit__(None, None, None)
            except Exception:
                pass

    return nc


_NC_CACHE = {}


def kernel(L0_attn_x, L0_attn_W_qkv, L0_ffn_weight):
    x = np.ascontiguousarray(np.asarray(L0_attn_x, dtype=np.float32))
    wq = np.ascontiguousarray(np.asarray(L0_attn_W_qkv, dtype=np.float32))
    wf = np.ascontiguousarray(np.asarray(L0_ffn_weight, dtype=np.float32))
    assert x.shape == (4, 4096, 1024)

    if "nc" not in _NC_CACHE:
        _NC_CACHE["nc"] = build_nc()
    nc = _NC_CACHE["nc"]

    in_maps = []
    for c in range(8):
        seq, half = c // 2, c % 2
        xs = np.ascontiguousarray(x[seq, half * TPC:(half + 1) * TPC, :])
        in_maps.append(dict(x=xs, wqkv=wq, wffn=wf))

    res = run_bass_kernel_spmd(nc, in_maps, core_ids=list(range(8)))
    out = np.empty((4, 4096, 1024), dtype=np.float32)
    for c in range(8):
        seq, half = c // 2, c % 2
        out[seq, half * TPC:(half + 1) * TPC, :] = res.results[c]["y"]
    return out
